# version 01
# baseline (speedup 1.0000x reference)
import contextlib
import math
import numpy as np
import concourse.bass as bass
import concourse.mybir as mybir
from concourse.bass_utils import run_bass_kernel_spmd

F32 = mybir.dt.float32
BF16 = mybir.dt.bfloat16
I32 = mybir.dt.int32
AF = mybir.ActivationFunctionType
ALU = mybir.AluOpType

NCORES = 8
B = 4
L = 4096
DM = 2048
DI = 4096
NST = 16
DTR = 128
INTER = 8192
DLOC = 512
TOWN = 2048

NSLAB2 = 16
SCH = 512
NCH = L // SCH
NQ = B * NCH

STAGE = 7
DEBUG_TAPS = False
USE_SWITCH = True
RSQ_DM = 1.0 / math.sqrt(DM)
RSQ_IN = 1.0 / math.sqrt(INTER)
RG = [list(range(NCORES))]


def _had(n):
    i = np.arange(n)
    bc = np.zeros((n, n), dtype=np.int64)
    a = (i[:, None] & i[None, :])
    while a.any():
        bc += a & 1
        a >>= 1
    return np.where(bc % 2 == 0, 1.0, -1.0).astype(np.float32)


class K:
    def __init__(self, nc, ctx):
        self.nc = nc
        self.ctx = ctx
        self._sems = {}
        self.cnt = {}

    def sem(self, name):
        if name not in self._sems:
            self._sems[name] = self.ctx.enter_context(self.nc.semaphore(name))
            self.cnt[name] = 0
        return self._sems[name]

    def inc(self, instr, name, n=1):
        instr.then_inc(self.sem(name), n)
        self.cnt[name] += n
        return self.cnt[name]

    def dma(self, eng, out, in_, name):
        return self.inc(eng.dma_start(out=out, in_=in_), name, 16)

    def wait(self, eng, name, val=None):
        v = self.cnt.get(name, 0) if val is None else val
        if v > 0:
            eng.wait_ge(self.sem(name), v)

    def get(self, name, default=0):
        return self.cnt.get(name, default)


def ln_vec_pre(e, k, tag, i, src, st, junk, nfeat):
    k.inc(e.tensor_scalar(junk[:], src, 1.0 / nfeat, 0.0, ALU.mult, ALU.add,
                          accum_out=st["mean"][i % 2][:]), f"{tag}m", 1)


def ln_vec_mid(e, k, tag, i, st, nfeat):
    k.wait(e, f"{tag}s", 2 * i + 1)
    k.wait(e, f"{tag}m", i + 1)
    e.tensor_scalar_mul(st["var"][i % 2][:], st["sq"][i % 2][:], 1.0 / nfeat)
    k.inc(e.scalar_tensor_tensor(st["m2"][i % 2][:], st["mean"][i % 2][:],
                                 st["mean"][i % 2][:], st["var"][i % 2][:],
                                 ALU.mult, ALU.subtract), f"{tag}v", 1)


def ln_vec_norm(e, k, tag, i, src, dst, st):
    k.wait(e, f"{tag}s", 2 * i + 2)
    return e.tensor_scalar(dst, src, st["mean"][i % 2][:], st["rstd"][i % 2][:],
                           ALU.subtract, ALU.mult)


def ln_sca_sq(e, k, tag, i, src, st, junk):
    k.inc(e.activation(out=junk[:], in_=src, func=AF.Square,
                       accum_out=st["sq"][i % 2][:]), f"{tag}s", 1)


def ln_sca_rstd(e, k, tag, i, st, eps_col):
    k.wait(e, f"{tag}v", i + 1)
    e.activation(out=st["lnv"][i % 2][:], in_=st["m2"][i % 2][:], func=AF.Ln,
                 scale=-1.0, bias=eps_col)
    k.inc(e.activation(out=st["rstd"][i % 2][:], in_=st["lnv"][i % 2][:],
                       func=AF.Exp, scale=-0.5), f"{tag}s", 1)


def mkst(sb, pre, where):
    return {nm: [sb(f"{pre}_{nm}{i}", [128, 1], where=where) for i in range(2)]
            for nm in ("mean", "sq", "var", "m2", "lnv", "rstd")}


def build_program():
    nc = bass.Bass("TRN2", target_bir_lowering=False)

    def din(name, shape, dt=F32):
        return nc.dram_tensor(name, list(shape), dt, kind="ExternalInput")

    x_own = din("x_own", [TOWN, DM])
    w_in_loc = din("w_in_loc", [DM, 1024])
    b_in_loc = din("b_in_loc", [1024, 1])
    convw_loc = din("convw_loc", [DLOC, 4])
    convb_loc = din("convb_loc", [DLOC, 1])
    wx_loc = din("wx_loc", [DLOC, 160])
    wdt_loc = din("wdt_loc", [DTR, DLOC])
    bdt_loc = din("bdt_loc", [DLOC, 1])
    alog_loc = din("alog_loc", [DLOC, NST])
    dpar_loc = din("dpar_loc", [DLOC, 1])
    w_out_in = din("w_out", [DI, DM])
    b_out_in = din("b_out", [128, 16])
    su_ug_in = din("su_ug", [128, 16])
    w_up_loc = din("w_up_loc", [1024, DM])
    w_gate_loc = din("w_gate_loc", [1024, DM])
    w_down_loc = din("w_down_loc", [256, INTER])
    sudp_loc = din("sudp_loc", [1024, 1])
    wsclup_loc = din("wsclup_loc", [1024, 1])
    wsclgate_loc = din("wsclgate_loc", [1024, 1])
    wscldp_loc = din("wscldp_loc", [256, 1])
    h128_in = din("h128", [128, 128])
    identf_in = din("identf", [128, 128])
    onesrow_in = din("onesrow", [1, 128])
    selmat_in = din("selmat", [16, 16 * 128])
    cst = din("cst", [128, 4])
    yfoff = din("yfoff", [1, 2], I32)

    out_t = nc.dram_tensor("out", [TOWN, DM], F32, kind="ExternalOutput")

    def internal(name, shape, dt, shared=False):
        return nc.dram_tensor(name, list(shape), dt, kind="Internal",
                              addr_space="Local")

    ag_xn_in = internal("ag_xn_in", [DM, TOWN], BF16)
    ag_xn_out = internal("ag_xn_out", [NCORES * DM, TOWN], BF16, shared=True)
    sp_co = internal("sp_co", [DLOC, B * L], BF16)
    sp_sg = internal("sp_sg", [DLOC, B * L], BF16)
    sp_dl = internal("sp_dl", [DLOC, B * L], BF16)
    ar_xp_in = internal("ar_xp_in", [160, B * L], F32)
    ar_xp_out = internal("ar_xp_out", [160, B * L], F32, shared=True)
    ag_yf_in = [internal(f"ag_yf_in{bb}", [DLOC, L], BF16) for bb in range(B)]
    ag_yf_out = [internal(f"ag_yf_out{bb}", [DI, L], BF16, shared=True) for bb in range(B)]
    ag_wup_in = internal("ag_wup_in", [DM, 1024], BF16)
    ag_wup_out = internal("ag_wup_out", [NCORES * DM, 1024], BF16, shared=True)
    ag_wg_in = internal("ag_wg_in", [DM, 1024], BF16)
    ag_wg_out = internal("ag_wg_out", [NCORES * DM, 1024], BF16, shared=True)
    ag_wdn_in = internal("ag_wdn_in", [INTER, 256], BF16)
    ag_wdn_out = internal("ag_wdn_out", [NCORES * INTER, 256], BF16, shared=True)
    wout_bf = internal("wout_bf", [DI, DM], BF16)
    yf_all = internal("yf_all", [B * DI, L], BF16)
    sp_out1 = internal("sp_out1", [TOWN, DM], F32)
    sp_x1 = internal("sp_x1", [DM, TOWN], BF16)
    sp_z2 = internal("sp_z2", [INTER, TOWN], BF16)

    taps = {}
    if DEBUG_TAPS:
        def tap(nm, shp, dt):
            taps[nm] = nc.dram_tensor(nm, list(shp), dt, kind="ExternalOutput")
        tap("tap_xn", [NCORES * DM, TOWN], BF16)
        tap("tap_co", [DLOC, B * L], BF16)
        tap("tap_sg", [DLOC, B * L], BF16)
        tap("tap_xp", [160, B * L], F32)
        tap("tap_dl", [DLOC, B * L], BF16)
        tap("tap_wib", [DM, 1024], BF16)
        tap("tap_wib2", [DM, 1024], BF16)
        tap("tap_hd", [DLOC, B * L], BF16)
        tap("tap_wup", [NCORES * DM, 1024], BF16)
        tap("tap_wdn", [NCORES * INTER, 256], BF16)
        if STAGE >= 4:
            for bb in range(B):
                tap(f"tap_yf{bb}", [DI, L], BF16)
        if STAGE >= 5:
            tap("tap_o1", [TOWN, DM], F32)
            tap("tap_x1", [DM, TOWN], BF16)
            tap("tap_mo", [128, 4 * 16 * 512], F32)
            tap("tap_yfs", [128, 4 * 32 * 512], BF16)
        if STAGE >= 6:
            tap("tap_z2", [INTER, TOWN], BF16)

    ctx = contextlib.ExitStack()
    with ctx:
        k = K(nc, ctx)

        def sb(name, shape, dt=F32, where=None):
            return (where or ctx).enter_context(nc.sbuf_tensor(name, shape, dt))

        def psum(name, shape, dt=F32, where=None):
            return (where or ctx).enter_context(nc.psum_tensor(name, shape, dt))

        def barrier():
            k.cnt["bar_epoch"] = k.get("bar_epoch") + 1
            tgt = 5 * k.cnt["bar_epoch"]
            bar = k.sem("bar")
            qld_t, qst_t = k.get("qld"), k.get("qst")
            with nc.Block() as blk:
                @blk.sync
                def _(e):
                    k.wait(e, "qld", qld_t)
                    e.sem_inc(bar, 1)
                    e.wait_ge(bar, tgt)

                @blk.scalar
                def _(e):
                    e.sem_inc(bar, 1)
                    e.wait_ge(bar, tgt)

                @blk.vector
                def _(e):
                    e.sem_inc(bar, 1)
                    e.wait_ge(bar, tgt)

                @blk.tensor
                def _(e):
                    e.sem_inc(bar, 1)
                    e.wait_ge(bar, tgt)

                @blk.gpsimd
                def _(e):
                    k.wait(e, "qst", qst_t)
                    e.sem_inc(bar, 1)
                    e.wait_ge(bar, tgt)

        t_A = [sb(f"t_A{i}", [128, NST]) for i in range(4)]
        t_dp = [sb(f"t_dp{i}", [128, 1]) for i in range(4)]
        t_hstar = [sb(f"t_hstar{i}", [128, 128], BF16) for i in range(16)]
        t_h128b = sb("t_h128b", [128, 128], BF16)
        t_identb = sb("t_identb", [128, 128], BF16)
        t_identf = sb("t_identf", [128, 128])
        t_onesb = sb("t_onesb", [1, 128], BF16)
        t_selb = sb("t_selb", [16, 16 * 128], BF16)
        t_selff = sb("t_selff", [16, 16 * 128])
        t_cst = sb("t_cst", [128, 4])
        t_bo = sb("t_bo", [128, 16])
        t_jv = sb("t_jv", [128, DM])
        t_js = sb("t_js", [128, DM])
        eps_col = t_cst[:, 0:1]

        sc0 = contextlib.ExitStack()
        ctx.enter_context(sc0)
        t_alog = [sb(f"t_alog{i}", [128, NST], where=sc0) for i in range(4)]
        t_su = sb("t_su", [128, 16], where=sc0)
        t_h128f = sb("t_h128f", [128, 128], where=sc0)
        t_bos = sb("t_bos", [128, 16], where=sc0)
        t_onesf = sb("t_onesf", [1, 128], where=sc0)
        t_self = sb("t_self", [16, 16 * 128], where=sc0)

        with nc.Block() as blk:
            @blk.sync
            def _(e):
                k.dma(e, t_cst[:], cst[:], "qld")
                k.dma(e, t_identf[:], identf_in[:], "qld")
                k.dma(e, t_h128f[:], h128_in[:], "qld")
                k.dma(e, t_onesf[:], onesrow_in[:], "qld")
                k.dma(e, t_self[:], selmat_in[:], "qld")
                k.dma(e, t_su[:], su_ug_in[:], "qld")
                k.dma(e, t_bos[:], b_out_in[:], "qld")
                for i in range(4):
                    k.dma(e, t_alog[i][:], alog_loc[i * 128:(i + 1) * 128, :], "qld")
                    k.dma(e, t_dp[i][:], dpar_loc[i * 128:(i + 1) * 128, :], "qld")
                k.cnt["p0ald"] = k.cnt["qld"]

            @blk.scalar
            def _(e):
                k.wait(e, "qld", k.cnt["p0ald"])
                for i in range(4):
                    k.inc(e.activation(out=t_A[i][:], in_=t_alog[i][:], func=AF.Exp), "p0s", 1)

            @blk.vector
            def _(e):
                k.wait(e, "qld", k.cnt["p0ald"])
                k.wait(e, "p0s", 4)
                for i in range(4):
                    e.tensor_scalar_mul(t_A[i][:], t_A[i][:], -1.0)
                e.tensor_copy(t_identb[:], t_identf[:])
                e.tensor_copy(t_h128b[:], t_h128f[:])
                e.tensor_copy(t_bo[:], t_bos[:])
                e.tensor_copy(t_onesb[:], t_onesf[:])
                e.tensor_copy(t_selb[:], t_self[:])
                e.tensor_copy(t_selff[:], t_self[:])
                for c in range(16):
                    e.tensor_scalar_mul(t_hstar[c][:], t_h128f[:], t_su[:, c:c + 1])
        barrier()

        sc1 = contextlib.ExitStack()
        ctx.enter_context(sc1)
        t_xt = [sb(f"t_xt{i}", [128, DM], where=sc1) for i in range(2)]
        t_xnt = sb("t_xnt", [128, DM], BF16, where=sc1)
        st1 = mkst(sb, "st1", sc1)
        t_stg1 = [sb(f"t_stg1{i}", [128, DM], BF16, where=sc1) for i in range(2)]
        ps_t1 = psum("ps_t1", [128, DM], BF16, where=sc1)

        qld0 = k.get("qld")
        qst0 = k.get("qst")
        with nc.Block() as blk:
            @blk.sync
            def _(e):
                for tt in range(16):
                    if tt >= 2:
                        k.wait(e, "p1n", tt - 1)
                    k.dma(e, t_xt[tt % 2][:], x_own[tt * 128:(tt + 1) * 128, :], "qld")

            @blk.vector
            def _(e):
                for tt in range(16):
                    r = tt % 2
                    k.wait(e, "qld", qld0 + 16 * (tt + 1))
                    ln_vec_pre(e, k, "l1", tt, t_xt[r][:], st1, t_jv, DM)
                    ln_vec_mid(e, k, "l1", tt, st1, DM)
                    k.inc(ln_vec_norm(e, k, "l1", tt, t_xt[r][:], t_xnt[:], st1), "p1n", 1)
                    k.wait(e, "p1t", tt + 1)
                    if tt >= 2:
                        k.wait(e, "qst", qst0 + 256 * (tt - 1))
                    k.inc(e.tensor_copy(t_stg1[r][:], ps_t1[:]), "p1e", 1)

            @blk.scalar
            def _(e):
                for tt in range(16):
                    k.wait(e, "qld", qld0 + 16 * (tt + 1))
                    ln_sca_sq(e, k, "l1", tt, t_xt[tt % 2][:], st1, t_js)
                    ln_sca_rstd(e, k, "l1", tt, st1, eps_col)

            @blk.tensor
            def _(e):
                for tt in range(16):
                    k.wait(e, "p1n", tt + 1)
                    if tt > 0:
                        k.wait(e, "p1e", tt)
                    for c in range(16):
                        ins = e.transpose(ps_t1[:, c * 128:(c + 1) * 128],
                                          t_xnt[:, c * 128:(c + 1) * 128], t_identb[:])
                    k.inc(ins, "p1t", 1)

            @blk.gpsimd
            def _(e):
                for tt in range(16):
                    k.wait(e, "p1e", tt + 1)
                    for c in range(16):
                        k.dma(e, ag_xn_in[c * 128:(c + 1) * 128, tt * 128:(tt + 1) * 128],
                              t_stg1[tt % 2][:, c * 128:(c + 1) * 128], "qst")
                k.wait(e, "qst", k.cnt["qst"])
                k.inc(e.collective_compute("AllGather", ALU.bypass, replica_groups=RG,
                                           ins=[ag_xn_in.ap().opt()],
                                           outs=[ag_xn_out.ap().opt()]), "scc", 1)
                k.cnt["cc_xn"] = k.cnt["scc"]
        barrier()
        sc1.close()

        t_wstg = [sb(f"t_wstg{i}", [128, INTER], where=sc0) for i in range(2)]
        t_wsc = [sb(f"t_wsc{i}", [128, 1], where=sc0) for i in range(2)]
        t_wbf = sb("t_wbf", [128, INTER], BF16, where=sc0)
        t_trs = sb("t_trs", [128, 2048], BF16, where=sc0)
        ps_tr0 = psum("ps_tr0", [128, 2048], BF16, where=sc0)

        def wprep(tag, rows, src, scls, pscale, agdst, ncc, butterfly):
            W = ncc * 128
            ngr = (ncc + 15) // 16
            mm0 = k.get(f"{tag}t")
            ev0 = k.get(f"{tag}e")
            v0 = k.get(f"{tag}v")
            q0 = k.get("qld")
            s0 = k.get("qst")
            nscl = len(scls)
            with nc.Block() as blk:
                @blk.sync
                def _(e):
                    k.wait(e, "scc", k.cnt["scc"])
                    for i in range(rows):
                        if i >= 1:
                            k.wait(e, f"{tag}v", v0 + i)
                        k.dma(e, t_wstg[i % 2][:, 0:W], src[i * 128:(i + 1) * 128, :], "qld")
                        for key, s in enumerate(scls):
                            k.dma(e, t_wsc[key][:], s[i * 128:(i + 1) * 128, :], "qld")

                @blk.vector
                def _(e):
                    for i in range(rows):
                        k.wait(e, "qld", q0 + 16 * (nscl + 1) * (i + 1))
                        cur = t_wstg[i % 2]
                        if butterfly:
                            a, bq = cur, t_wstg[(i + 1) % 2]
                            for s in range(6):
                                wdt = (1 << s) * 128
                                va = a[:, 0:W].rearrange("p (g x) -> p g x", x=2 * wdt)
                                vb = bq[:, 0:W].rearrange("p (g x) -> p g x", x=2 * wdt)
                                e.tensor_tensor(vb[:, :, 0:wdt], va[:, :, 0:wdt],
                                                va[:, :, wdt:2 * wdt], ALU.add)
                                e.tensor_tensor(vb[:, :, wdt:2 * wdt], va[:, :, 0:wdt],
                                                va[:, :, wdt:2 * wdt], ALU.subtract)
                                a, bq = bq, a
                            cur = a
                        if nscl == 2:
                            e.tensor_scalar(cur[:, 0:W], cur[:, 0:W], t_wsc[0][:], pscale,
                                            ALU.mult, ALU.mult)
                            k.inc(e.tensor_scalar_mul(t_wbf[:, 0:W], cur[:, 0:W],
                                                      t_wsc[1][:]), f"{tag}v", 1)
                        else:
                            k.inc(e.tensor_scalar(t_wbf[:, 0:W], cur[:, 0:W], t_wsc[0][:],
                                                  pscale, ALU.mult, ALU.mult), f"{tag}v", 1)
                        for g in range(ngr):
                            lo, hi = g * 16, min(g * 16 + 16, ncc)
                            k.wait(e, f"{tag}t", mm0 + i * ngr + g + 1)
                            prev = i * ngr + g - 1
                            if prev >= 0:
                                k.wait(e, "qst", s0 + 16 * min(prev * 16 + 16, (prev // ngr) * ncc + min((prev % ngr) * 16 + 16, ncc)))
                            k.inc(e.tensor_copy(t_trs[:, 0:(hi - lo) * 128],
                                                ps_tr0[:, 0:(hi - lo) * 128]), f"{tag}e", 1)

                @blk.tensor
                def _(e):
                    for i in range(rows):
                        k.wait(e, f"{tag}v", v0 + i + 1)
                        for g in range(ngr):
                            lo, hi = g * 16, min(g * 16 + 16, ncc)
                            prev = i * ngr + g - 1
                            if prev >= 0:
                                k.wait(e, f"{tag}e", ev0 + prev + 1)
                            for j in range(lo, hi):
                                ins = e.transpose(ps_tr0[:, (j - lo) * 128:(j - lo + 1) * 128],
                                                  t_wbf[:, j * 128:(j + 1) * 128], t_identb[:])
                            k.inc(ins, f"{tag}t", 1)

                @blk.gpsimd
                def _(e):
                    for i in range(rows):
                        for g in range(ngr):
                            lo, hi = g * 16, min(g * 16 + 16, ncc)
                            k.wait(e, f"{tag}e", ev0 + i * ngr + g + 1)
                            for j in range(lo, hi):
                                k.dma(e, agdst[j * 128:(j + 1) * 128, i * 128:(i + 1) * 128],
                                      t_trs[:, (j - lo) * 128:(j - lo + 1) * 128], "qst")

        wprep("wu", 8, w_up_loc, [sudp_loc, wsclup_loc], RSQ_DM, ag_wup_in, 16, False)
        barrier()
        wprep("wg", 8, w_gate_loc, [wsclgate_loc], RSQ_DM, ag_wg_in, 16, False)
        barrier()
        wprep("wd", 2, w_down_loc, [wscldp_loc], RSQ_IN, ag_wdn_in, 64, True)
        barrier()

        t_wof = [sb(f"t_wof{i}", [128, DM], where=sc0) for i in range(2)]
        t_wob = [sb(f"t_wob{i}", [128, DM], BF16, where=sc0) for i in range(2)]
        qld0 = k.get("qld")
        qst0 = k.get("qst")
        with nc.Block() as blk:
            @blk.sync
            def _(e):
                for i in range(32):
                    if i >= 2:
                        k.wait(e, "wocast", i - 1)
                    k.dma(e, t_wof[i % 2][:], w_out_in[i * 128:(i + 1) * 128, :], "qld")

            @blk.vector
            def _(e):
                for i in range(32):
                    k.wait(e, "qld", qld0 + 16 * (i + 1))
                    if i >= 2:
                        k.wait(e, "qst", qst0 + 16 * (i - 1))
                    k.inc(e.tensor_copy(t_wob[i % 2][:], t_wof[i % 2][:]), "wocast", 1)

            @blk.gpsimd
            def _(e):
                for i in range(32):
                    k.wait(e, "wocast", i + 1)
                    k.dma(e, wout_bf[i * 128:(i + 1) * 128, :], t_wob[i % 2][:], "qst")
                k.wait(e, "qst", k.cnt["qst"])
                for (iin, iout) in [(ag_wup_in, ag_wup_out), (ag_wg_in, ag_wg_out),
                                    (ag_wdn_in, ag_wdn_out)]:
                    k.inc(e.collective_compute("AllGather", ALU.bypass, replica_groups=RG,
                                               ins=[iin.ap().opt()],
                                               outs=[iout.ap().opt()]), "scc", 1)
                k.cnt["cc_w"] = k.cnt["scc"]
        barrier()
        sc0.close()

        sc2 = contextlib.ExitStack()
        ctx.enter_context(sc2)
        t_wif = [sb(f"t_wif{i}", [128, 1024], where=sc2) for i in range(2)]
        w_in_bf = [sb(f"w_in_bf{i}", [128, 1024], BF16, where=sc2) for i in range(16)]
        t_wdtf = sb("t_wdtf", [128, DLOC], where=sc2)
        wdt_bf = sb("wdt_bf", [128, DLOC], BF16, where=sc2)
        t_wxf = [sb(f"t_wxf{i}", [128, 160], where=sc2) for i in range(4)]
        wx_bf = [sb(f"wx_bf{i}", [128, 160], BF16, where=sc2) for i in range(4)]
        t_cw = [sb(f"t_cw{i}", [128, 4], where=sc2) for i in range(4)]
        t_cb = [sb(f"t_cb{i}", [128, 1], where=sc2) for i in range(4)]
        t_binh = [sb(f"t_binh{i}", [128, 1], where=sc2) for i in range(4)]
        t_bing = [sb(f"t_bing{i}", [128, 1], where=sc2) for i in range(4)]
        t_bdt = [sb(f"t_bdt{i}", [128, 1], where=sc2) for i in range(4)]

        qld0 = k.get("qld")
        with nc.Block() as blk:
            @blk.sync
            def _(e):
                k.wait(e, "scc", k.cnt["scc"])
                for i in range(16):
                    if i >= 1:
                        k.wait(e, "wic", i)
                    k.dma(e, t_wif[i % 2][:], w_in_loc[i * 128:(i + 1) * 128, :], "qld")
                k.dma(e, t_wdtf[:], wdt_loc[:], "qld")
                for i in range(4):
                    k.dma(e, t_wxf[i][:], wx_loc[i * 128:(i + 1) * 128, :], "qld")
                    k.dma(e, t_cw[i][:], convw_loc[i * 128:(i + 1) * 128, :], "qld")
                    k.dma(e, t_cb[i][:], convb_loc[i * 128:(i + 1) * 128, :], "qld")
                    k.dma(e, t_binh[i][:], b_in_loc[i * 128:(i + 1) * 128, :], "qld")
                    k.dma(e, t_bing[i][:], b_in_loc[512 + i * 128:512 + (i + 1) * 128, :], "qld")
                    k.dma(e, t_bdt[i][:], bdt_loc[i * 128:(i + 1) * 128, :], "qld")

            @blk.vector
            def _(e):
                for i in range(16):
                    k.wait(e, "qld", qld0 + 16 * (i + 1))
                    k.inc(e.tensor_copy(w_in_bf[i][:], t_wif[i % 2][:]), "wic", 1)
                k.wait(e, "qld", qld0 + 16 * 16 + 25 * 16)
                e.tensor_copy(wdt_bf[:], t_wdtf[:])
                for i in range(4):
                    e.tensor_copy(wx_bf[i][:], t_wxf[i][:])
        barrier()
        if DEBUG_TAPS:
            with nc.Block() as blk:
                @blk.gpsimd
                def _(e):
                    for i in range(16):
                        k.dma(e, taps["tap_wib2"][i * 128:(i + 1) * 128, :], w_in_bf[i][:], "qst")
                    k.wait(e, "qst", k.cnt["qst"])
            barrier()

        t_xns = [sb(f"t_xns{i}", [128, 16 * 1024], BF16, where=sc2) for i in range(2)]
        t_hx = [sb(f"t_hx{i}", [128, 1028], BF16, where=sc2) for i in range(4)]
        t_halo = [sb(f"t_halo{i}", [128, 3], BF16, where=sc2) for i in range(4)]
        t_ca = [sb(f"t_ca{i}", [128, 1024], BF16, where=sc2) for i in range(2)]
        t_co2 = [sb(f"t_co2{i}", [128, 1024], BF16, where=sc2) for i in range(4)]
        t_sgt = [sb(f"t_sgt{i}", [128, 1024], BF16, where=sc2) for i in range(2)]
        t_xpa = sb("t_xpa", [128, 1024], where=sc2)
        t_xpb = sb("t_xpb", [32, 1024], where=sc2)
        sc2p = contextlib.ExitStack()
        ctx.enter_context(sc2p)
        ps2 = [psum(f"ps2_{i}", [128, 1024], where=sc2p) for i in range(4)]

        qld0 = k.get("qld")
        qst0 = k.get("qst")
        with nc.Block() as blk:
            @blk.sync
            def _(e):
                k.wait(e, "scc", k.cnt["cc_xn"])
                for j in range(NSLAB2):
                    r, sub = j // 2, j % 2
                    if j >= 2:
                        k.wait(e, "p2mm", 9 * (j - 1))
                    for c in range(16):
                        k.dma(e, t_xns[j % 2][:, c * 1024:(c + 1) * 1024],
                              ag_xn_out[r * 2048 + c * 128:r * 2048 + (c + 1) * 128,
                                        sub * 1024:(sub + 1) * 1024], "qld")

            @blk.tensor
            def _(e):
                for j in range(NSLAB2):
                    k.wait(e, "qld", qld0 + 256 * (j + 1))
                    xs = t_xns[j % 2]
                    k.wait(e, "p2hev", 4 * j)
                    k.wait(e, "p2gev", 4 * j)
                    k.wait(e, "p2xpe", j)
                    for m in range(4):
                        for hh in range(2):
                            for kk in range(16):
                                ins = e.matmul(ps2[m][:, hh * 512:(hh + 1) * 512],
                                               w_in_bf[kk][:, m * 128:(m + 1) * 128],
                                               xs[:, kk * 1024 + hh * 512:kk * 1024 + (hh + 1) * 512],
                                               start=(kk == 0), stop=(kk == 15))
                        k.inc(ins, "p2mm", 1)
                    for m in range(4, 8):
                        k.wait(e, "p2hev", 4 * j + (m - 3))
                        for hh in range(2):
                            for kk in range(16):
                                ins = e.matmul(ps2[m - 4][:, hh * 512:(hh + 1) * 512],
                                               w_in_bf[kk][:, m * 128:(m + 1) * 128],
                                               xs[:, kk * 1024 + hh * 512:kk * 1024 + (hh + 1) * 512],
                                               start=(kk == 0), stop=(kk == 15))
                        k.inc(ins, "p2mm", 1)
                    k.wait(e, "p2gev", 4 * j + 2)
                    for hh in range(2):
                        for ch in range(4):
                            if hh == 0:
                                k.wait(e, "p2co", 4 * j + ch + 1)
                            e.matmul(ps2[0][:, hh * 512:(hh + 1) * 512],
                                     wx_bf[ch][:, 0:128],
                                     t_co2[ch][:, hh * 512:(hh + 1) * 512],
                                     start=(ch == 0), stop=(ch == 3))
                            ins = e.matmul(ps2[1][0:32, hh * 512:(hh + 1) * 512],
                                           wx_bf[ch][:, 128:160],
                                           t_co2[ch][:, hh * 512:(hh + 1) * 512],
                                           start=(ch == 0), stop=(ch == 3))
                    k.inc(ins, "p2mm", 1)

            @blk.vector
            def _(e):
                for j in range(NSLAB2):
                    for ch in range(4):
                        k.wait(e, "p2mm", 9 * j + ch + 1)
                        if DEBUG_TAPS and j > 0:
                            k.wait(e, "qsc", 64 * j)
                        if j % 4 == 0:
                            e.memset(t_halo[ch][:], 0)
                        e.tensor_copy(t_hx[ch][:, 1:4], t_halo[ch][:])
                        k.inc(e.tensor_scalar_add(t_hx[ch][:, 4:1028], ps2[ch][:],
                                                  t_binh[ch][:]), "p2hev", 1)
                        lconv = 4 * j + ch
                        if lconv >= 2:
                            k.wait(e, "p2co", lconv - 1)
                        ca = t_ca[ch % 2]
                        e.tensor_scalar_mul(ca[:], t_hx[ch][:, 1:1025], t_cw[ch][:, 0:1])
                        for kk in range(1, 4):
                            e.scalar_tensor_tensor(ca[:], t_hx[ch][:, 1 + kk:1025 + kk],
                                                   t_cw[ch][:, kk:kk + 1], ca[:],
                                                   ALU.mult, ALU.add)
                        k.inc(e.tensor_copy(t_halo[ch][:], t_hx[ch][:, 1025:1028]), "p2cv", 1)
                    k.wait(e, "p2mm", 9 * j + 9)
                    if j > 0:
                        k.wait(e, "qst", qst0 + 160 * j)
                    e.tensor_copy(t_xpa[:], ps2[0][:])
                    k.inc(e.tensor_copy(t_xpb[:], ps2[1][0:32, :]), "p2xpe", 1)

            @blk.scalar
            def _(e):
                for j in range(NSLAB2):
                    for ch in range(4):
                        k.wait(e, "p2cv", 4 * j + ch + 1)
                        if DEBUG_TAPS:
                            k.dma(e, taps["tap_hd"][ch * 128:(ch + 1) * 128,
                                                    j * 1024:(j + 1) * 1024],
                                  t_hx[ch][:, 4:1028], "qsc")
                        if j > 0:
                            k.wait(e, "qst", qst0 + 160 * (j - 1) + 16 * (ch + 1))
                            k.wait(e, "p2mm", 9 * (j - 1) + 9)
                        k.inc(e.activation(out=t_co2[ch][:], in_=t_ca[ch % 2][:],
                                           func=AF.Silu, bias=t_cb[ch][:]), "p2co", 1)
                    for m in range(4, 8):
                        k.wait(e, "p2mm", 9 * j + m + 1)
                        lg = 4 * j + (m - 4)
                        if lg >= 2:
                            pj, pm = (lg - 2) // 4, (lg - 2) % 4
                            k.wait(e, "qst", qst0 + 160 * pj + 64 + 16 * (pm + 1))
                        k.inc(e.activation(out=t_sgt[m % 2][:], in_=ps2[m - 4][:],
                                           func=AF.Silu, bias=t_bing[m - 4][:]), "p2gev", 1)

            @blk.gpsimd
            def _(e):
                for j in range(NSLAB2):
                    g0 = j * 1024
                    for ch in range(4):
                        k.wait(e, "p2co", 4 * j + ch + 1)
                        k.dma(e, sp_co[ch * 128:(ch + 1) * 128, g0:g0 + 1024], t_co2[ch][:], "qst")
                    for m in range(4):
                        k.wait(e, "p2gev", 4 * j + m + 1)
                        k.dma(e, sp_sg[m * 128:(m + 1) * 128, g0:g0 + 1024],
                              t_sgt[m % 2][:], "qst")
                    k.wait(e, "p2xpe", j + 1)
                    k.dma(e, ar_xp_in[0:128, g0:g0 + 1024], t_xpa[:], "qst")
                    k.dma(e, ar_xp_in[128:160, g0:g0 + 1024], t_xpb[:], "qst")
                k.wait(e, "qst", k.cnt["qst"])
                k.inc(e.collective_compute("AllReduce", ALU.add, replica_groups=RG,
                                           ins=[ar_xp_in.ap().opt()],
                                           outs=[ar_xp_out.ap().opt()]), "scc", 1)
                k.cnt["cc_xp"] = k.cnt["scc"]
        barrier()
        sc2p.close()

        t_dtf = [sb(f"t_dtf{i}", [128, 1024], where=sc2) for i in range(2)]
        t_dtb = [sb(f"t_dtb{i}", [128, 1024], BF16, where=sc2) for i in range(2)]
        t_s1 = sb("t_s1", [128, 1024], where=sc2)
        t_sab = sb("t_sab", [128, 1024], where=sc2)
        t_dlt = [sb(f"t_dlt{i}", [128, 1024], BF16, where=sc2) for i in range(2)]
        ps_d3 = [psum(f"ps_d3{i}", [128, 1024], where=sc2) for i in range(2)]

        qld0 = k.get("qld")
        qst0 = k.get("qst")
        with nc.Block() as blk:
            @blk.sync
            def _(e):
                k.wait(e, "scc", k.cnt["cc_xp"])
                for j in range(NSLAB2):
                    if j >= 2:
                        k.wait(e, "p3b", j - 1)
                    k.dma(e, t_dtf[j % 2][:], ar_xp_out[0:128, j * 1024:(j + 1) * 1024], "qld")

            @blk.vector
            def _(e):
                for j in range(NSLAB2):
                    k.wait(e, "qld", qld0 + 16 * (j + 1))
                    k.inc(e.tensor_copy(t_dtb[j % 2][:], t_dtf[j % 2][:]), "p3b", 1)
                    for dm in range(4):
                        i = 4 * j + dm
                        k.wait(e, "p3mm", i + 1)
                        if i > 0:
                            k.wait(e, "p3l", i)
                        k.inc(e.tensor_scalar_add(t_s1[:], ps_d3[dm % 2][:],
                                                   t_bdt[dm][:]), "p3s1", 1)
                        k.wait(e, "p3l", i + 1)
                        if i >= 2:
                            k.wait(e, "qst", qst0 + 16 * (i - 1))
                        k.inc(e.scalar_tensor_tensor(t_dlt[dm % 2][:], t_s1[:], 0.0,
                                                     t_sab[:], ALU.max, ALU.add), "p3c", 1)

            @blk.tensor
            def _(e):
                for j in range(NSLAB2):
                    k.wait(e, "p3b", j + 1)
                    for dm in range(4):
                        i = 4 * j + dm
                        if i >= 2:
                            k.wait(e, "p3s1", i - 1)
                        for hh in range(2):
                            ins = e.matmul(ps_d3[dm % 2][:, hh * 512:(hh + 1) * 512],
                                           wdt_bf[:, dm * 128:(dm + 1) * 128],
                                           t_dtb[j % 2][:, hh * 512:(hh + 1) * 512],
                                           start=True, stop=True)
                        k.inc(ins, "p3mm", 1)

            @blk.scalar
            def _(e):
                for j in range(NSLAB2):
                    for dm in range(4):
                        i = 4 * j + dm
                        k.wait(e, "p3s1", i + 1)
                        if i > 0:
                            k.wait(e, "p3c", i)
                        e.activation(out=t_sab[:], in_=t_s1[:], func=AF.Abs)
                        e.activation(out=t_sab[:], in_=t_sab[:], func=AF.Exp, scale=-1.0)
                        k.inc(e.activation(out=t_sab[:], in_=t_sab[:], func=AF.Ln,
                                           bias=1.0), "p3l", 1)

            @blk.gpsimd
            def _(e):
                for j in range(NSLAB2):
                    for dm in range(4):
                        i = 4 * j + dm
                        k.wait(e, "p3c", i + 1)
                        k.dma(e, sp_dl[dm * 128:(dm + 1) * 128, j * 1024:(j + 1) * 1024],
                              t_dlt[dm % 2][:], "qst")
        barrier()
        sc2.close()

        if DEBUG_TAPS:
            with nc.Block() as blk:
                @blk.gpsimd
                def _(e):
                    k.dma(e, taps["tap_co"].ap(), sp_co.ap(), "qst")
                    k.dma(e, taps["tap_sg"].ap(), sp_sg.ap(), "qst")
                    k.dma(e, taps["tap_xp"].ap(), ar_xp_out.ap(), "qst")
                    k.dma(e, taps["tap_dl"].ap(), sp_dl.ap(), "qst")
                    for i in range(16):
                        k.dma(e, taps["tap_wib"][i * 128:(i + 1) * 128, :], w_in_bf[i][:], "qst")
                    k.dma(e, taps["tap_xn"].ap(), ag_xn_out.ap(), "qst")
                    k.wait(e, "scc", k.get("cc_w"))
                    k.dma(e, taps["tap_wup"].ap(), ag_wup_out.ap(), "qst")
                    k.dma(e, taps["tap_wdn"].ap(), ag_wdn_out.ap(), "qst")
                    k.wait(e, "qst", k.cnt["qst"])
            barrier()

        if STAGE >= 4:
            sc4 = contextlib.ExitStack()
            ctx.enter_context(sc4)
            t_dl4 = [[sb(f"t_dl4_{r}_{d2}", [128, SCH], BF16, where=sc4) for d2 in range(4)]
                     for r in range(2)]
            t_co4 = [[sb(f"t_co4_{r}_{d2}", [128, SCH], BF16, where=sc4) for d2 in range(4)]
                     for r in range(2)]
            t_sg4 = [[sb(f"t_sg4_{r}_{d2}", [128, SCH], BF16, where=sc4) for d2 in range(4)]
                     for r in range(2)]
            t_bfl = [sb(f"t_bfl{r}", [16, SCH], where=sc4) for r in range(2)]
            t_cfl = [sb(f"t_cfl{r}", [16, SCH], where=sc4) for r in range(2)]

            t_Bb = [sb(f"t_Bb{r}", [128, 16 * SCH], BF16, where=sc4) for r in range(2)]
            t_Cb = [sb(f"t_Cb{r}", [128, 16 * SCH], BF16, where=sc4) for r in range(2)]
            t_dx = sb("t_dx", [128, SCH], BF16, where=sc4)
            t_u = sb("t_u", [128, 16 * SCH], BF16, where=sc4)
            t_h = sb("t_h", [128, 16 * SCH], BF16, where=sc4)
            t_hl = [sb(f"t_hl{d2}", [128, NST], where=sc4) for d2 in range(4)]
            t_da = [sb(f"t_da{r}", [128, SCH], where=sc4) for r in range(2)]
            t_yt = sb("t_yt", [128, 16 * SCH], BF16, where=sc4)
            t_t4 = sb("t_t4", [128, 8 * SCH], BF16, where=sc4)
            t_t2 = sb("t_t2", [128, 4 * SCH], BF16, where=sc4)
            t_t1 = sb("t_t1", [128, 2 * SCH], BF16, where=sc4)
            t_y = sb("t_y", [128, SCH], BF16, where=sc4)
            t_cd = sb("t_cd", [128, SCH], BF16, where=sc4)
            t_yf = [sb(f"t_yf{r}", [128, SCH], BF16, where=sc4) for r in range(2)]
            ps_bc = [psum(f"ps_bc{r}", [128, 512], where=sc4) for r in range(2)]

            hv = t_h[:].rearrange("p (n t) -> p n t", n=16)
            uv = t_u[:].rearrange("p (n t) -> p n t", n=16)
            qld0 = k.get("qld")
            qst0 = k.get("qst")
            scc0 = k.get("scc")

            with nc.Block() as blk:
                @blk.sync
                def _(e):
                    for q in range(NQ):
                        bq, ch = q // NCH, q % NCH
                        g0 = bq * L + ch * SCH
                        if ch == 0 and bq > 0:
                            k.wait(e, "scc", scc0 + bq)
                        if q >= 2:
                            k.wait(e, "p4bm", 32 * (q - 1))
                            k.wait(e, "p4yf", 4 * (q - 1))
                        r = q % 2
                        for d2 in range(4):
                            k.dma(e, t_dl4[r][d2][:], sp_dl[d2 * 128:(d2 + 1) * 128, g0:g0 + SCH], "qld")
                            k.dma(e, t_co4[r][d2][:], sp_co[d2 * 128:(d2 + 1) * 128, g0:g0 + SCH], "qld")
                            k.dma(e, t_sg4[r][d2][:], sp_sg[d2 * 128:(d2 + 1) * 128, g0:g0 + SCH], "qld")
                        k.dma(e, t_bfl[r][:], ar_xp_out[128:144, g0:g0 + SCH], "qld")
                        k.dma(e, t_cfl[r][:], ar_xp_out[144:160, g0:g0 + SCH], "qld")

                @blk.vector
                def _(e):
                    for q in range(NQ):
                        bq, ch = q // NCH, q % NCH
                        r = q % 2
                        k.wait(e, "qld", qld0 + 224 * (q + 1))
                        for t8 in range(16):
                            k.wait(e, "p4bm", 32 * q + t8 + 1)
                            k.inc(e.tensor_copy(t_Bb[r][:, t8 * 512:(t8 + 1) * 512],
                                                ps_bc[t8 % 2][:]), "p4be", 1)
                        for t8 in range(16):
                            k.wait(e, "p4bm", 32 * q + 16 + t8 + 1)
                            k.inc(e.tensor_copy(t_Cb[r][:, t8 * 512:(t8 + 1) * 512],
                                                ps_bc[t8 % 2][:]), "p4be", 1)
                        for d2 in range(4):
                            g = 4 * q + d2
                            e.tensor_mul(t_dx[:], t_dl4[r][d2][:], t_co4[r][d2][:])
                            dxb = t_dx[:].rearrange("p (o t) -> p o t", o=1).broadcast_to((128, 16, SCH))
                            e.tensor_tensor(uv, dxb,
                                            t_Bb[r][:].rearrange("p (n t) -> p n t", n=16), ALU.mult)
                            for n in range(NST):
                                k.wait(e, "p4ex", 16 * g + n + 1)
                                init = 0.0 if ch == 0 else t_hl[d2][:, n:n + 1]
                                k.inc(e.tensor_tensor_scan(hv[:, n, :], t_da[n % 2][:],
                                                           uv[:, n, :], init,
                                                           ALU.mult, ALU.add), "p4sc", 1)
                            k.inc(e.tensor_copy(t_hl[d2][:], hv[:, :, SCH - 1]), "p4sc", 1)
                            e.tensor_tensor(t_yt[:], t_h[:], t_Cb[r][:], ALU.mult)
                            e.tensor_add(t_t4[:], t_yt[:, 0:8 * SCH], t_yt[:, 8 * SCH:16 * SCH])
                            e.tensor_add(t_t2[:], t_t4[:, 0:4 * SCH], t_t4[:, 4 * SCH:8 * SCH])
                            e.tensor_add(t_t1[:], t_t2[:, 0:2 * SCH], t_t2[:, 2 * SCH:4 * SCH])
                            e.tensor_add(t_y[:], t_t1[:, 0:SCH], t_t1[:, SCH:2 * SCH])
                            e.tensor_scalar_mul(t_cd[:], t_co4[r][d2][:], t_dp[d2][:])
                            e.tensor_add(t_cd[:], t_cd[:], t_y[:])
                            if g >= 2:
                                k.wait(e, "qst", qst0 + 16 * (g - 1) + 16 * ((g - 2) // 32))
                            k.inc(e.tensor_mul(t_yf[d2 % 2][:], t_cd[:], t_sg4[r][d2][:]),
                                  "p4yf", 1)

                @blk.scalar
                def _(e):
                    for q in range(NQ):
                        r = q % 2
                        k.wait(e, "qld", qld0 + 224 * (q + 1))
                        for d2 in range(4):
                            g = 4 * q + d2
                            for n in range(NST):
                                i = 16 * g + n
                                if i >= 2:
                                    k.wait(e, "p4sc", 17 * g + n - 1)
                                k.inc(e.activation(out=t_da[n % 2][:], in_=t_dl4[r][d2][:],
                                                   func=AF.Exp, scale=t_A[d2][:, n:n + 1]), "p4ex", 1)

                @blk.tensor
                def _(e):
                    for q in range(NQ):
                        r = q % 2
                        k.wait(e, "qld", qld0 + 224 * (q + 1))
                        for t8 in range(16):
                            i = 32 * q + t8
                            if i >= 2:
                                k.wait(e, "p4be", i - 1)
                            k.inc(e.matmul(ps_bc[t8 % 2][:],
                                           t_selff[:, t8 * 128:(t8 + 1) * 128],
                                           t_bfl[r][:], start=True, stop=True), "p4bm", 1)
                        for t8 in range(16):
                            i = 32 * q + 16 + t8
                            k.wait(e, "p4be", i - 1)
                            k.inc(e.matmul(ps_bc[t8 % 2][:],
                                           t_selff[:, t8 * 128:(t8 + 1) * 128],
                                           t_cfl[r][:], start=True, stop=True), "p4bm", 1)

                @blk.gpsimd
                def _(e):
                    for q in range(NQ):
                        bq, ch = q // NCH, q % NCH
                        if ch == 0 and bq > 0:
                            k.wait(e, "scc", scc0 + bq)
                        for d2 in range(4):
                            k.wait(e, "p4yf", 4 * q + d2 + 1)
                            k.dma(e, ag_yf_in[bq][d2 * 128:(d2 + 1) * 128, ch * SCH:(ch + 1) * SCH],
                                  t_yf[d2 % 2][:], "qst")
                        if ch == NCH - 1:
                            k.wait(e, "qst", k.cnt["qst"])
                            k.inc(e.collective_compute("AllGather", ALU.bypass, replica_groups=RG,
                                                       ins=[ag_yf_in[bq].ap().opt()],
                                                       outs=[ag_yf_out[bq].ap().opt()]), "scc", 1)
                            k.cnt[f"cc_yf{bq}"] = k.cnt["scc"]
                            k.wait(e, "scc", k.cnt["scc"])
                            k.dma(e, yf_all[bq * DI:(bq + 1) * DI, :], ag_yf_out[bq].ap(), "qst")
            barrier()
            sc4.close()

            if DEBUG_TAPS:
                with nc.Block() as blk:
                    @blk.gpsimd
                    def _(e):
                        k.wait(e, "scc", k.cnt["scc"])
                        for bb in range(B):
                            k.dma(e, taps[f"tap_yf{bb}"].ap(), ag_yf_out[bb].ap(), "qst")
                        k.wait(e, "qst", k.cnt["qst"])
                barrier()

        if STAGE >= 5:
            sc5 = contextlib.ExitStack()
            ctx.enter_context(sc5)
            t_yfs = sb("t_yfs", [128, 32 * 512], BF16, where=sc5)
            t_wo5 = [sb(f"t_wo5{r}", [128, 32 * 128], BF16, where=sc5) for r in range(2)]
            t_mo = sb("t_mo", [128, 16 * 512], where=sc5)
            t_moT = sb("t_moT", [128, DM], where=sc5)
            t_nr = sb("t_nr", [128, DM], where=sc5)
            t_o1 = sb("t_o1", [128, DM], where=sc5)
            t_vT = sb("t_vT", [128, DM], where=sc5)
            t_xo = [sb(f"t_xo{r}", [128, DM], where=sc5) for r in range(2)]
            t_vfs = [sb(f"t_vfs{c}", [128, 512], BF16, where=sc5) for c in range(16)]
            t_w0 = [sb(f"t_w0{c}", [128, 512], BF16, where=sc5) for c in range(16)]
            st5a = mkst(sb, "st5a", sc5)
            st5b = mkst(sb, "st5b", sc5)
            ps8 = psum("ps8", [128, 4096], where=sc5)

            qld0 = k.get("qld")
            qst0 = k.get("qst")
            N5 = 4
            t_yoff = sb("t_yoff", [1, 2], I32, where=sc5)

            with nc.Block() as blk:
                @blk.sync
                def _(e):
                    k.wait(e, "scc", k.get(f"cc_yf{B - 1}"))
                    k.dma(e, t_yoff[:], yfoff[:], "qld")
                    k.wait(e, "qld", qld0 + 16)
                    rbase = e.value_load(t_yoff[0:1, 0:1])
                    with e.register("r_yf") as r_yf:
                        for s in range(N5):
                            if s >= 1:
                                k.wait(e, "p5mm", 16 * s)
                                if DEBUG_TAPS:
                                    k.wait(e, "qsc", k.get("qsc5", 0) + 32 * (s - 1) + 16)
                            for kq in range(4):
                                e.reg_add(r_yf, rbase, kq * 8 * 128 * L + s * 512)
                                ap = bass.AP(yf_all, r_yf,
                                             [[L, 128], [128 * L, 8], [1, 512]])
                                k.dma(e, t_yfs[:].rearrange(
                                    "p (a t) -> p a t", a=32)[:, kq * 8:(kq + 1) * 8, :],
                                    ap, "qld")
                            for m in range(16):
                                if m >= 2:
                                    k.wait(e, "p5mm", 16 * s + m - 1)
                                k.dma(e, t_wo5[m % 2][:],
                                      bass.AP(wout_bf, m * 128,
                                              [[DM, 128], [128 * DM, 32], [1, 128]]), "qld")
                            for tt in range(4):
                                i5 = 4 * s + tt
                                if i5 >= 2:
                                    k.wait(e, "p5o1", i5 - 1)
                                k.dma(e, t_xo[tt % 2][:],
                                      x_own[s * 512 + tt * 128:s * 512 + (tt + 1) * 128, :], "qld")

                @blk.tensor
                def _(e):
                    for s in range(N5):
                        for m in range(16):
                            k.wait(e, "qld", qld0 + 16 + 384 * s + 64 + 16 * (m + 1))
                            if s > 0 and m == 0:
                                k.wait(e, "p5he", 16 * s)
                            i5g = 16 * s + m
                            if i5g >= 8:
                                k.wait(e, "p5ev", i5g - 7)
                            for kk in range(32):
                                ins = e.matmul(ps8[:, (m % 8) * 512:(m % 8 + 1) * 512],
                                               t_wo5[m % 2][:, kk * 128:(kk + 1) * 128],
                                               t_yfs[:, kk * 512:(kk + 1) * 512],
                                               start=(kk == 0), stop=(kk == 31))
                            k.inc(ins, "p5mm", 1)
                        for tt in range(4):
                            i5 = 4 * s + tt
                            k.wait(e, "p5ev", 16 * (s + 1))
                            if i5 >= 1:
                                k.wait(e, "p5ve", i5)
                            for c in range(16):
                                ins = e.transpose(
                                    ps8[:, (tt % 2) * 2048 + c * 128:(tt % 2) * 2048 + (c + 1) * 128],
                                    t_mo[:, c * 512 + tt * 128:c * 512 + (tt + 1) * 128],
                                    t_identf[:])
                            k.inc(ins, "p5tr", 1)
                            k.wait(e, "p5v3", i5 + 1)
                            for c in range(16):
                                ins = e.transpose(
                                    ps8[:, (tt % 2) * 2048 + c * 128:(tt % 2) * 2048 + (c + 1) * 128],
                                    t_vT[:, c * 128:(c + 1) * 128], t_identf[:])
                            k.inc(ins, "p5vt", 1)
                        for c in range(16):
                            k.wait(e, "p5ve", 4 * (s + 1))
                            i5c = 16 * s + c
                            if i5c >= 8:
                                k.wait(e, "p5he", i5c - 7)
                            k.inc(e.matmul(ps8[:, (c % 8) * 512:(c % 8 + 1) * 512],
                                           t_hstar[c][:], t_vfs[c][:],
                                           start=True, stop=True), "p5hm", 1)

                @blk.vector
                def _(e):
                    for s in range(N5):
                        for m in range(16):
                            k.wait(e, "p5mm", 16 * s + m + 1)
                            if DEBUG_TAPS and s > 0 and m == 0:
                                k.wait(e, "qsc", k.get("qsc5", 0) + 32 * s)
                            k.inc(e.tensor_scalar_add(t_mo[:, m * 512:(m + 1) * 512],
                                                      ps8[:, (m % 8) * 512:(m % 8 + 1) * 512],
                                                      t_bo[:, m:m + 1]), "p5ev", 1)
                        for tt in range(4):
                            i5 = 4 * s + tt
                            k.wait(e, "p5tr", i5 + 1)
                            if i5 >= 1:
                                k.wait(e, "l2s", 2 * i5 - 1)
                            k.inc(e.tensor_copy(t_moT[:],
                                                ps8[:, (tt % 2) * 2048:(tt % 2) * 2048 + 2048]),
                                  "p5te", 1)
                            ln_vec_pre(e, k, "l2", i5, t_moT[:], st5a, t_jv, DM)
                            ln_vec_mid(e, k, "l2", i5, st5a, DM)
                            ln_vec_norm(e, k, "l2", i5, t_moT[:], t_nr[:], st5a)
                            k.wait(e, "qld", qld0 + 16 + 384 * s + 320 + 16 * (tt + 1))
                            if i5 >= 1:
                                k.wait(e, "qst", qst0 + 320 * s + 16 * tt)
                                k.wait(e, "l3s", 2 * i5 - 1)
                            k.inc(e.tensor_add(t_o1[:], t_nr[:], t_xo[tt % 2][:]), "p5o1", 1)
                            ln_vec_pre(e, k, "l3", i5, t_o1[:], st5b, t_jv, DM)
                            ln_vec_mid(e, k, "l3", i5, st5b, DM)
                            k.inc(ln_vec_norm(e, k, "l3", i5, t_o1[:], t_vT[:], st5b), "p5v3", 1)
                            k.wait(e, "p5vt", i5 + 1)
                            for c in range(16):
                                ins = e.tensor_copy(t_vfs[c][:, tt * 128:(tt + 1) * 128],
                                                    ps8[:, (tt % 2) * 2048 + c * 128:(tt % 2) * 2048 + (c + 1) * 128])
                            k.inc(ins, "p5ve", 1)
                        for c in range(16):
                            k.wait(e, "p5hm", 16 * s + c + 1)
                            if s >= 1:
                                k.wait(e, "qst", qst0 + 320 * s)
                            k.inc(e.tensor_copy(t_w0[c][:], ps8[:, (c % 8) * 512:(c % 8 + 1) * 512]),
                                  "p5he", 1)
                        src, dst = t_w0, t_vfs
                        for stg in range(4):
                            bit = 1 << stg
                            for i in range(16):
                                if i & bit:
                                    continue
                                j2 = i | bit
                                e.tensor_add(dst[i][:], src[i][:], src[j2][:])
                                ins = e.tensor_sub(dst[j2][:], src[i][:], src[j2][:])
                            src, dst = dst, src
                        k.inc(ins, "p5bf", 1)

                @blk.scalar
                def _(e):
                    k.cnt["qsc5"] = k.get("qsc")
                    for s in range(N5):
                        if DEBUG_TAPS:
                            k.wait(e, "qld", qld0 + 16 + 384 * s + 64)
                            k.dma(e, taps["tap_yfs"][:, s * 16384:(s + 1) * 16384], t_yfs[:], "qsc")
                        for tt in range(4):
                            i5 = 4 * s + tt
                            k.wait(e, "p5te", i5 + 1)
                            ln_sca_sq(e, k, "l2", i5, t_moT[:], st5a, t_js)
                            ln_sca_rstd(e, k, "l2", i5, st5a, eps_col)
                            k.wait(e, "p5o1", i5 + 1)
                            ln_sca_sq(e, k, "l3", i5, t_o1[:], st5b, t_js)
                            ln_sca_rstd(e, k, "l3", i5, st5b, eps_col)
                        if DEBUG_TAPS:
                            k.wait(e, "p5ev", 16 * (s + 1))
                            k.dma(e, taps["tap_mo"][:, s * 8192:(s + 1) * 8192], t_mo[:], "qsc")

                @blk.gpsimd
                def _(e):
                    for s in range(N5):
                        for tt in range(4):
                            i5 = 4 * s + tt
                            k.wait(e, "p5o1", i5 + 1)
                            k.dma(e, sp_out1[s * 512 + tt * 128:s * 512 + (tt + 1) * 128, :],
                                  t_o1[:], "qst")
                        k.wait(e, "p5bf", s + 1)
                        for c in range(16):
                            k.dma(e, sp_x1[c * 128:(c + 1) * 128, s * 512:(s + 1) * 512],
                                  t_w0[c][:], "qst")
            barrier()
            sc5.close()

            if DEBUG_TAPS:
                with nc.Block() as blk:
                    @blk.gpsimd
                    def _(e):
                        k.dma(e, taps["tap_o1"].ap(), sp_out1.ap(), "qst")
                        k.dma(e, taps["tap_x1"].ap(), sp_x1.ap(), "qst")
                        k.wait(e, "qst", k.cnt["qst"])
                barrier()

        if STAGE >= 6:
            sc6 = contextlib.ExitStack()
            ctx.enter_context(sc6)
            t_x1f = [sb(f"t_x1f{c}", [128, TOWN], BF16, where=sc6) for c in range(16)]
            t_wu6 = [sb(f"t_wu6{r}", [128, 16 * 128], BF16, where=sc6) for r in range(2)]
            t_wg6 = [sb(f"t_wg6{r}", [128, 16 * 128], BF16, where=sc6) for r in range(2)]
            t_up = sb("t_up", [128, TOWN], BF16, where=sc6)
            t_sg6 = sb("t_sg6", [128, TOWN], BF16, where=sc6)
            t_z = sb("t_z", [128, TOWN], BF16, where=sc6)
            t_z2 = [sb(f"t_z2{r}", [128, TOWN], BF16, where=sc6) for r in range(2)]
            ps_up = psum("ps_up", [128, 2048], where=sc6)
            ps_g = psum("ps_g", [128, 2048], where=sc6)

            qld0 = k.get("qld")
            qst0 = k.get("qst")
            with nc.Block() as blk:
                @blk.sync
                def _(e):
                    k.wait(e, "scc", k.get("cc_w"))
                    for c in range(16):
                        k.dma(e, t_x1f[c][:], sp_x1[c * 128:(c + 1) * 128, :], "qld")
                    for m in range(64):
                        if m >= 2:
                            k.wait(e, "p6mm", m - 1)
                        jb = m // 8
                        k.dma(e, t_wu6[m % 2][:],
                              bass.AP(ag_wup_out, jb * 2048 * 1024 + (m % 8) * 128,
                                      [[1024, 128], [128 * 1024, 16], [1, 128]]), "qld")
                        k.dma(e, t_wg6[m % 2][:],
                              bass.AP(ag_wg_out, jb * 2048 * 1024 + (m % 8) * 128,
                                      [[1024, 128], [128 * 1024, 16], [1, 128]]), "qld")

                @blk.tensor
                def _(e):
                    for m in range(64):
                        k.wait(e, "qld", qld0 + 256 + 32 * (m + 1))
                        if m > 0:
                            k.wait(e, "p6ze", m)
                        for hh in range(4):
                            for kk in range(16):
                                ins = e.matmul(ps_up[:, hh * 512:(hh + 1) * 512],
                                               t_wu6[m % 2][:, kk * 128:(kk + 1) * 128],
                                               t_x1f[kk][:, hh * 512:(hh + 1) * 512],
                                               start=(kk == 0), stop=(kk == 15))
                        k.inc(ins, "p6um", 1)
                        if m > 0:
                            k.wait(e, "p6sg", m)
                        for hh in range(4):
                            for kk in range(16):
                                ins = e.matmul(ps_g[:, hh * 512:(hh + 1) * 512],
                                               t_wg6[m % 2][:, kk * 128:(kk + 1) * 128],
                                               t_x1f[kk][:, hh * 512:(hh + 1) * 512],
                                               start=(kk == 0), stop=(kk == 15))
                        k.inc(ins, "p6mm", 1)
                        k.wait(e, "p6z", m + 1)
                        k.wait(e, "p6ue", m + 1)
                        for hh in range(4):
                            ins = e.matmul(ps_up[:, hh * 512:(hh + 1) * 512], t_h128b[:],
                                           t_z[:, hh * 512:(hh + 1) * 512],
                                           start=True, stop=True)
                        k.inc(ins, "p6m2", 1)

                @blk.vector
                def _(e):
                    for m in range(64):
                        k.wait(e, "p6um", m + 1)
                        k.inc(e.tensor_copy(t_up[:], ps_up[:]), "p6ue", 1)
                        k.wait(e, "p6sg", m + 1)
                        k.inc(e.tensor_mul(t_z[:], t_sg6[:], t_up[:]), "p6z", 1)
                        k.wait(e, "p6m2", m + 1)
                        if m >= 2:
                            k.wait(e, "qst", qst0 + 16 * (m - 1))
                        k.inc(e.tensor_copy(t_z2[m % 2][:], ps_up[:]), "p6ze", 1)

                @blk.scalar
                def _(e):
                    for m in range(64):
                        k.wait(e, "p6mm", m + 1)
                        if m > 0:
                            k.wait(e, "p6z", m)
                        k.inc(e.activation(out=t_sg6[:], in_=ps_g[:], func=AF.Silu), "p6sg", 1)

                @blk.gpsimd
                def _(e):
                    for m in range(64):
                        k.wait(e, "p6ze", m + 1)
                        k.dma(e, sp_z2[m * 128:(m + 1) * 128, :], t_z2[m % 2][:], "qst")
            barrier()
            sc6.close()

            if DEBUG_TAPS:
                with nc.Block() as blk:
                    @blk.gpsimd
                    def _(e):
                        k.dma(e, taps["tap_z2"].ap(), sp_z2.ap(), "qst")
                        k.wait(e, "qst", k.cnt["qst"])
                barrier()

        if STAGE >= 7:
            sc7 = contextlib.ExitStack()
            ctx.enter_context(sc7)
            t_z2s = sb("t_z2s", [128, 64 * 512], BF16, where=sc7)
            t_wd7 = [sb(f"t_wd7{r}", [128, 64 * 128], BF16, where=sc7) for r in range(2)]
            t_fb = sb("t_fb", [128, 16 * 512], where=sc7)
            t_fT = sb("t_fT", [128, DM], where=sc7)
            t_nr7 = sb("t_nr7", [128, DM], where=sc7)
            t_o1r = [sb(f"t_o1r{r}", [128, DM], where=sc7) for r in range(2)]
            t_fin = [sb("t_fin0", [128, DM], where=sc7)]
            st7 = mkst(sb, "st7", sc7)
            ps7 = psum("ps7", [128, 4096], where=sc7)

            qld0 = k.get("qld")
            qst0 = k.get("qst")
            N5 = 4
            with nc.Block() as blk:
                @blk.sync
                def _(e):
                    for s in range(N5):
                        if s >= 1:
                            k.wait(e, "p7mm", 16 * s)
                        for kq in range(4):
                            k.dma(e, t_z2s[:].rearrange("p (a t) -> p a t", a=64)[:, kq * 16:(kq + 1) * 16, :],
                                  bass.AP(sp_z2, kq * 16 * 128 * TOWN + s * 512,
                                          [[TOWN, 128], [128 * TOWN, 16], [1, 512]]), "qld")
                        for m2 in range(16):
                            if m2 >= 2:
                                k.wait(e, "p7mm", 16 * s + m2 - 1)
                            jb = m2 // 2
                            k.dma(e, t_wd7[m2 % 2][:],
                                  bass.AP(ag_wdn_out, jb * INTER * 256 + (m2 % 2) * 128,
                                          [[256, 128], [128 * 256, 64], [1, 128]]), "qld")
                        for tt in range(4):
                            i7 = 4 * s + tt
                            if i7 >= 2:
                                k.wait(e, "p7fn", i7 - 1)
                            k.dma(e, t_o1r[tt % 2][:],
                                  sp_out1[s * 512 + tt * 128:s * 512 + (tt + 1) * 128, :], "qld")

                @blk.tensor
                def _(e):
                    for s in range(N5):
                        k.wait(e, "qld", qld0 + 384 * s + 64)
                        for m2 in range(16):
                            k.wait(e, "qld", qld0 + 384 * s + 64 + 16 * (m2 + 1))
                            i7g = 16 * s + m2
                            if i7g >= 8:
                                k.wait(e, "p7ev", i7g - 7)
                            if s > 0 and m2 == 0:
                                k.wait(e, "p7te", 4 * s)
                            for kk in range(64):
                                ins = e.matmul(ps7[:, (m2 % 8) * 512:(m2 % 8 + 1) * 512],
                                               t_wd7[m2 % 2][:, kk * 128:(kk + 1) * 128],
                                               t_z2s[:, kk * 512:(kk + 1) * 512],
                                               start=(kk == 0), stop=(kk == 63))
                            k.inc(ins, "p7mm", 1)
                        for tt in range(4):
                            i7 = 4 * s + tt
                            k.wait(e, "p7ev", 16 * (s + 1))
                            if i7 >= 1:
                                k.wait(e, "p7te", i7)
                            for c in range(16):
                                ins = e.transpose(
                                    ps7[:, (tt % 2) * 2048 + c * 128:(tt % 2) * 2048 + (c + 1) * 128],
                                    t_fb[:, c * 512 + tt * 128:c * 512 + (tt + 1) * 128],
                                    t_identf[:])
                            k.inc(ins, "p7tr", 1)

                @blk.vector
                def _(e):
                    for s in range(N5):
                        for m2 in range(16):
                            k.wait(e, "p7mm", 16 * s + m2 + 1)
                            k.inc(e.tensor_copy(t_fb[:, m2 * 512:(m2 + 1) * 512],
                                                ps7[:, (m2 % 8) * 512:(m2 % 8 + 1) * 512]), "p7ev", 1)
                        for tt in range(4):
                            i7 = 4 * s + tt
                            k.wait(e, "p7tr", i7 + 1)
                            if i7 >= 1:
                                k.wait(e, "l4s", 2 * i7 - 1)
                            k.inc(e.tensor_copy(t_fT[:],
                                                ps7[:, (tt % 2) * 2048:(tt % 2) * 2048 + 2048]),
                                  "p7te", 1)
                            ln_vec_pre(e, k, "l4", i7, t_fT[:], st7, t_jv, DM)
                            ln_vec_mid(e, k, "l4", i7, st7, DM)
                            ln_vec_norm(e, k, "l4", i7, t_fT[:], t_nr7[:], st7)
                            k.wait(e, "qld", qld0 + 384 * s + 320 + 16 * (tt + 1))
                            if i7 >= 1:
                                k.wait(e, "qst", qst0 + 16 * i7)
                            k.inc(e.tensor_add(t_fin[0][:], t_nr7[:], t_o1r[tt % 2][:]),
                                  "p7fn", 1)

                @blk.scalar
                def _(e):
                    for s in range(N5):
                        for tt in range(4):
                            i7 = 4 * s + tt
                            k.wait(e, "p7te", i7 + 1)
                            ln_sca_sq(e, k, "l4", i7, t_fT[:], st7, t_js)
                            ln_sca_rstd(e, k, "l4", i7, st7, eps_col)

                @blk.gpsimd
                def _(e):
                    for s in range(N5):
                        for tt in range(4):
                            i7 = 4 * s + tt
                            k.wait(e, "p7fn", i7 + 1)
                            k.dma(e, out_t[s * 512 + tt * 128:s * 512 + (tt + 1) * 128, :],
                                  t_fin[0][:], "qst")
            barrier()
            sc7.close()

        with nc.Block() as blk:
            @blk.gpsimd
            def _(e):
                k.wait(e, "qst")
                k.wait(e, "scc")
    return nc


_CACHE = {}


def _shard_inputs(inputs):
    x = inputs["L0_attn_hidden_states"]
    W_in = inputs["L0_attn_W_in"]
    b_in = inputs["L0_attn_b_in"]
    conv_w = inputs["L0_attn_conv_w"]
    conv_b = inputs["L0_attn_conv_b"]
    W_x = inputs["L0_attn_W_x"]
    W_dt = inputs["L0_attn_W_dt"]
    b_dt = inputs["L0_attn_b_dt"]
    W_out = inputs["L0_attn_W_out"]
    b_out = inputs["L0_attn_b_out"]
    a_log = inputs["L0_attn_a_log"]
    d_par = inputs["L0_attn_d_param"]
    SU_ug = inputs["L0_ffn_SU_ug"]
    SU_dp = inputs["L0_ffn_SU_dp"]
    Wscl_ug = inputs["L0_ffn_Wscale_ug"]
    Wscl_dp = inputs["L0_ffn_Wscale_dp"]
    W_up = inputs["L0_ffn_W_up"]
    W_gate = inputs["L0_ffn_W_gate"]
    W_down = inputs["L0_ffn_W_down"]

    H = _had(128)
    identf = np.eye(128, dtype=np.float32)
    cstv = np.zeros((128, 4), dtype=np.float32)
    cstv[:, 0] = 1e-5
    onesrow = np.ones((1, 128), dtype=np.float32)
    selmat = np.zeros((16, 16 * 128), dtype=np.float32)
    for n in range(16):
        selmat[n, n * 128:(n + 1) * 128] = 1.0

    def C(a):
        return np.ascontiguousarray(a, dtype=np.float32)

    maps = []
    for c in range(NCORES):
        b, p = c // 2, c % 2
        ds = slice(c * DLOC, (c + 1) * DLOC)
        isl = slice(c * 1024, (c + 1) * 1024)
        msl = slice(c * 256, (c + 1) * 256)
        maps.append({
            "x_own": C(x[b, p * TOWN:(p + 1) * TOWN]),
            "w_in_loc": C(np.concatenate([W_in[:, ds], W_in[:, DI:][:, ds]], axis=1)),
            "b_in_loc": C(np.concatenate([b_in[ds], b_in[DI:][ds]]).reshape(-1, 1)),
            "convw_loc": C(conv_w[:, ds].T),
            "convb_loc": C(conv_b[ds].reshape(-1, 1)),
            "wx_loc": C(W_x[ds]),
            "wdt_loc": C(W_dt[:, ds]),
            "bdt_loc": C(b_dt[ds].reshape(-1, 1)),
            "alog_loc": C(a_log[ds]),
            "dpar_loc": C(d_par[ds].reshape(-1, 1)),
            "w_out": C(W_out),
            "b_out": C(b_out.reshape(16, 128).T),
            "su_ug": C(SU_ug.reshape(16, 128).T),
            "w_up_loc": C(W_up[isl]),
            "w_gate_loc": C(W_gate[isl]),
            "w_down_loc": C(W_down[msl]),
            "sudp_loc": C(SU_dp[isl].reshape(-1, 1)),
            "wsclup_loc": C(Wscl_ug[isl].reshape(-1, 1)),
            "wsclgate_loc": C(Wscl_ug[INTER:][isl].reshape(-1, 1)),
            "wscldp_loc": C(Wscl_dp[msl].reshape(-1, 1)),
            "h128": H,
            "identf": identf,
            "onesrow": onesrow,
            "selmat": selmat,
            "cst": cstv,
            "yfoff": np.array([[b * DI * L + p * TOWN, 0]], dtype=np.int32),
        })
    return maps


def kernel(**inputs):
    maps = _shard_inputs(inputs)
    key = (STAGE, DEBUG_TAPS)
    if key not in _CACHE:
        _CACHE[key] = build_program()
    nc = _CACHE[key]
    res = run_bass_kernel_spmd(nc, maps, core_ids=list(range(NCORES)))
    outp = np.zeros((B, L, DM), dtype=np.float32)
    for c in range(NCORES):
        b, p = c // 2, c % 2
        outp[b, p * TOWN:(p + 1) * TOWN] = np.asarray(res.results[c]["out"])
    kernel.last_results = res
    return outp
